# version 03 (final)
# baseline (speedup 1.0000x reference)
import math
import numpy as np
import ml_dtypes
import concourse.bass as bass
import concourse.mybir as mybir
from concourse.bass_utils import run_bass_kernel_spmd

F32 = mybir.dt.float32
BF16 = mybir.dt.bfloat16
AF = mybir.ActivationFunctionType
ALU = mybir.AluOpType
AX = mybir.AxisListType
MAGIC = float(1.5 * 2 ** 23)

T = 8192
D = 1024
NH = 8
CH = 128
II = 2816
NI = II // 128
G = 512
NG = T // G
NT = T // 128
EPS = 1e-5
QMAX = 127.0
NBUCK = 32
MAXDIST = 128
MU = math.sqrt(0.5)
STD = math.sqrt(0.25 * math.pi)
CERF = 1.0 / (STD * math.sqrt(2.0))
SQD = math.sqrt(128.0)

GAMMAS = [1.0 - 2.0 ** (-5.0 - h) for h in range(NH)]


def _bucket(d):
    n = max(d, 0)
    if n < NBUCK // 2:
        return n
    v = (NBUCK // 2) + int(math.log(max(n, 1) / (NBUCK // 2)) / math.log(MAXDIST / (NBUCK // 2)) * (NBUCK - NBUCK // 2))
    return min(v, NBUCK - 1)


def host_constants():
    c = {}
    c["ident"] = np.eye(128, dtype=ml_dtypes.bfloat16)
    dt_all = np.zeros((NH, 128, 128), np.float32)
    dq_all = np.zeros((NH, 128, 128), np.float32)
    dk_all = np.zeros((128, NH), np.float32)
    for h, g in enumerate(GAMMAS):
        i = np.arange(128)
        diff = i[None, :] - i[:, None]
        dt_all[h] = np.where(diff >= 0, g ** np.maximum(diff, 0), 0.0)
        dq_all[h] = (g ** (i + 1.0))[None, :] * np.ones((128, 1))
        dk_all[:, h] = g ** (127.0 - i)
    c["dt_all"] = dt_all
    c["dq_all"] = dq_all.astype(ml_dtypes.bfloat16)
    c["dk_all"] = dk_all
    masks = np.zeros((NBUCK, 128, 256), np.float32)
    pj = np.arange(128)[:, None]
    pi = np.arange(128)[None, :]
    b0 = np.vectorize(_bucket)(pi - pj)
    b1 = np.vectorize(_bucket)(128 + pi - pj)
    for b in range(NBUCK):
        masks[b, :, 0:128] = (b0 == b)
        masks[b, :, 128:256] = (b1 == b)
    c["bmasks"] = masks.astype(ml_dtypes.bfloat16)
    c["mdiag"] = (pi >= pj).astype(ml_dtypes.bfloat16)
    return c


class Buf:
    __slots__ = ("last_w", "readers", "name")

    def __init__(self, name=""):
        self.last_w = None
        self.readers = []
        self.name = name


class Plan:
    STREAMS = {
        "tensor": ("tensor", False),
        "vector": ("vector", False),
        "scalar": ("scalar", False),
        "gpsimd": ("gpsimd", False),
        "sync": ("sync", False),
        "qsync": ("sync", True),
        "qscalar": ("scalar", True),
        "qgp": ("gpsimd", True),
    }

    def __init__(self):
        self.ops = {s: [] for s in self.STREAMS}
        self.order = []

    def last(self, stream):
        n = len(self.ops[stream])
        return (stream, n - 1) if n else None

    def op(self, stream, emit, deps=(), reads=(), writes=(), preads=(), laggy=False):
        idx = len(self.ops[stream])
        dep = [d for d in deps if d is not None]
        force = set()
        for b in preads:
            if b.last_w is not None:
                dep.append(b.last_w)
                force.add(b.last_w)
        for b in reads:
            if b.last_w is not None:
                dep.append(b.last_w)
        for b in writes:
            if b.last_w is not None:
                dep.append(b.last_w)
            dep.extend(b.readers)
        ev = (stream, idx)
        self.ops[stream].append({"emit": emit, "deps": dep, "mark": False,
                                 "laggy": laggy, "force": force})
        self.order.append(ev)
        for b in preads:
            b.readers.append(ev)
        for b in reads:
            b.readers.append(ev)
        for b in writes:
            b.last_w = ev
            b.readers = []
        return ev

    def barrier(self):
        deps = [self.last(s) for s in self.STREAMS]
        deps = [d for d in deps if d is not None]
        for s in ("tensor", "vector", "scalar", "gpsimd", "sync"):
            self.op(s, lambda e: e.nop(), deps=deps)

    def finalize(self, nc):
        eng_of = {s: e for s, (e, _) in self.STREAMS.items()}
        for s in self.STREAMS:
            for o in self.ops[s]:
                for (ds, di) in o["deps"]:
                    if ds == s and self.STREAMS[ds][1]:
                        continue
                    self.ops[ds][di]["mark"] = True
        for s, (e_, is_dma) in self.STREAMS.items():
            if is_dma:
                for o in self.ops[s]:
                    o["mark"] = True
        self.cum = {}
        for s in self.STREAMS:
            cnt = 0
            cums = []
            for o in self.ops[s]:
                if o["mark"]:
                    cnt += 1
                cums.append(cnt)
            self.cum[s] = cums
        self.sems = {}
        self._ctxs = []
        for s in self.STREAMS:
            ctx = nc.semaphore(f"sem_{s}")
            self.sems[s] = ctx.__enter__()
            self._ctxs.append(ctx)

    def emit_engine(self, engobj, engname):
        eng_of = {s: e for s, (e, _) in self.STREAMS.items()}
        mystreams = {s for s in self.STREAMS if eng_of[s] == engname}
        hiwater = {}
        for (s, i) in self.order:
            if s not in mystreams:
                continue
            o = self.ops[s][i]
            needs = {}
            for (ds, di) in o["deps"]:
                if ds == s and self.STREAMS[ds][1]:
                    continue
                v = self.cum[ds][di]
                if self.STREAMS[ds][1]:
                    v *= 16
                if v > needs.get(ds, 0):
                    needs[ds] = v
            for ds, v in needs.items():
                if hiwater.get(ds, 0) >= v:
                    continue
                engobj.wait_ge(self.sems[ds], v)
                hiwater[ds] = v
            inst = o["emit"](engobj)
            if o["mark"]:
                inst.then_inc(self.sems[s], 16 if self.STREAMS[s][1] else 1)
            if TRACE_EMIT is not None:
                TRACE_EMIT.write(f"{s}[{i}] waits={sorted(needs.items())} inc={o['mark']} :: {str(inst)[:140]}\n")


class Arena:
    def __init__(self, tensor, cols):
        self.t = tensor
        self.cols = cols
        self.off = 0
        self.marks = []
        self.peak = 0

    def push(self):
        self.marks.append(self.off)

    def pop(self):
        self.off = self.marks.pop()

    LOG = []

    def bf16(self, ncols, name=""):
        Arena.LOG.append((name, self.off, ncols))
        pad = (ncols + 15) // 16 * 16
        assert self.off + pad <= self.cols, f"arena overflow {name}: {self.off}+{pad}>{self.cols}"
        ap = self.t[:, self.off:self.off + ncols]
        self.off += pad
        self.peak = max(self.peak, self.off)
        return ap, Buf(name)

    def f32(self, ncols, name=""):
        Arena.LOG.append((name + ":f32", self.off, 2 * ncols))
        pad = (2 * ncols + 15) // 16 * 16
        assert self.off + pad <= self.cols, f"arena overflow {name}: {self.off}+{pad}>{self.cols}"
        ap = self.t[:, self.off:self.off + 2 * ncols].bitcast(F32)
        self.off += pad
        self.peak = max(self.peak, self.off)
        return ap, Buf(name)


PARTS = 7
TRACE_EMIT = None
DETECT_RACES = True
NCORES = 8
GROUP_BARRIER = 0
SLAB_BARRIER = 1
DBG = 0

def build_program(stop_after=4):
    nc = bass.Bass("TRN2", target_bir_lowering=False, detect_race_conditions=DETECT_RACES)
    P = Plan()

    def din(name, shape, dt=F32):
        return nc.dram_tensor(name, shape, dt, kind="ExternalInput").ap()

    x0 = din("x0", [T, D])
    wq = din("wq", [D, D]); wk = din("wk", [D, D]); wv = din("wv", [D, D]); wo = din("wo", [D, D])
    wqk = din("wqk", [D, 128]); bqk = din("bqk", [128])
    gmm = din("gmm", [2, 128]); bet = din("bet", [2, 128])
    wvb = din("wvb", [D, D]); bvb = din("bvb", [D])
    rel = din("rel", [NBUCK, 1])
    clipx = din("clipx", [1]); clipd = din("clipd", [1])
    wg = din("wg", [II, D]); wu = din("wu", [II, D]); wd = din("wd", [D, II])
    ident_d = din("ident", [128, 128], BF16)
    dt_d = din("dt_all", [NH, 128, 128])
    dq_d = din("dq_all", [NH, 128, 128], BF16)
    dk_d = din("dk_all", [128, NH])
    bm_d = din("bmasks", [NBUCK, 128, 256], BF16)
    md_d = din("mdiag", [128, 128], BF16)

    out_d = nc.dram_tensor("out", [T, D], F32, kind="ExternalOutput").ap()
    dbg_d = nc.dram_tensor("dbg", [128, 49152], BF16, kind="ExternalOutput").ap() if DBG else None
    dbg_col = [0]

    def dump(ap, buf, cols):
        if not DBG:
            return
        c0 = dbg_col[0]
        dbg_col[0] += cols
        assert dbg_col[0] <= 49152
        P.op("qsync", lambda g, ap=ap, c0=c0, cols=cols: g.dma_start(
            out=dbg_d[:, c0:c0 + cols], in_=ap), reads=(buf,))
    xA = nc.dram_tensor("xA", [T, D], F32).ap()
    xB = nc.dram_tensor("xB", [T, D], F32).ap()
    xC = nc.dram_tensor("xC", [T, D], F32).ap()
    h_scr = nc.dram_tensor("h_scr", [II, T], F32).ap()
    v_scr = nc.dram_tensor("v_scr", [NT, 128, D], BF16).ap()
    wgt_c = nc.dram_tensor("wgt_c", [8, 128, II], BF16).ap()
    wut_c = nc.dram_tensor("wut_c", [8, 128, II], BF16).ap()
    wdt_c = nc.dram_tensor("wdt_c", [NI, 128, D], BF16).ap()
    cc_in = nc.dram_tensor("cc_in", [1], F32).ap()
    cc_out = nc.dram_tensor("cc_out", [1], F32).ap()
    bc_scr = nc.dram_tensor("bc_scr", [2], F32).ap()

    from contextlib import ExitStack
    stack = ExitStack()
    e = stack.enter_context

    AR_COLS = 101888
    arena_t = e(nc.sbuf_tensor("arena", [128, AR_COLS], BF16))
    A = Arena(arena_t, AR_COLS)

    pers_t = e(nc.sbuf_tensor("pers", [128, 1024], F32))
    pcol = [0]

    def pers_f32(n, name=""):
        s = pers_t[:, pcol[0]:pcol[0] + n]
        pcol[0] += n
        return s, Buf(name)

    id_sb, id_b = pers_f32(64, "ident")
    id_sb = id_sb.bitcast(BF16)
    dk_sb, dk_b = pers_f32(NH, "dk")
    md_sb, md_b = pers_f32(64, "mdiag"); md_sb = md_sb.bitcast(BF16)
    rel32_sb, rel32_b = pers_f32(NBUCK, "rel32")
    sig_sb, sig_b = pers_f32(2, "sig")
    swg_sb, swg_b = pers_f32(NI, "swg")
    swu_sb, swu_b = pers_f32(NI, "swu")
    gb_sb, gb_b = pers_f32(6, "gammabeta")
    tiny_sb, tiny_b = pers_f32(16, "tiny")
    bfar_sb, bfar_b = pers_f32(1, "bfar")
    muc_sb, muc_b = pers_f32(1, "muc")
    MV, _mvp = pers_f32(2 * NT, "mv")
    RSTD, _rsp = pers_f32(NT, "rstd")
    RS2, _rs2p = pers_f32(NT, "rs2")
    RW, _rwp = pers_f32(32, "rw")
    GSC, _g1 = pers_f32(NI, "gsc")
    USC, _g2 = pers_f32(NI, "usc")
    SXV, _g3 = pers_f32(2, "sxv")
    SHV, _g4 = pers_f32(2, "shv")
    XMAX, _g5 = pers_f32(4, "xmax")
    HMAX, _g6 = pers_f32(4, "hmax")

    PB = [e(nc.psum_tensor(f"pb{i}", [128, 512], F32)) for i in range(6)]
    PBb = [Buf(f"pb{i}") for i in range(6)]
    PT = e(nc.psum_tensor("pt", [128, 1024], BF16))
    PTb = Buf("pt")
    PT2 = e(nc.psum_tensor("pt2", [128, 1024], BF16))
    PT2b = Buf("pt2")
    PTs = (PT, PT2)
    PTbs = (PTb, PT2b)

    def dma(q, out, in_, reads=(), writes=(), deps=()):
        return P.op(q, lambda g, out=out, in_=in_: g.dma_start(out=out, in_=in_),
                    deps=deps, reads=reads, writes=writes)

    def ts(out, in0, s1, s2, op0, op1=None, reads=(), writes=(), deps=(), preads=()):
        if op1 is None:
            return P.op("vector", lambda v: v.tensor_scalar(out, in0, s1, None, op0, ALU.bypass),
                        deps=deps, reads=reads, writes=writes, preads=preads)
        return P.op("vector", lambda v: v.tensor_scalar(out, in0, s1, s2, op0, op1),
                    deps=deps, reads=reads, writes=writes, preads=preads)

    def tt(out, in0, in1, op, reads=(), writes=(), deps=()):
        return P.op("vector", lambda v: v.tensor_tensor(out, in0, in1, op),
                    deps=deps, reads=reads, writes=writes)

    def stt(out, in0, scalar, in1, op0, op1, reads=(), writes=(), deps=(), preads=()):
        return P.op("vector", lambda v: v.scalar_tensor_tensor(out, in0, scalar, in1, op0, op1),
                    deps=deps, reads=reads, writes=writes, preads=preads)

    def act(out, in_, func, bias=0.0, scale=1.0, reads=(), writes=(), deps=(), accum_out=None, laggy=False):
        return P.op("scalar", lambda s: s.activation(out, in_, func, bias=bias, scale=scale,
                                                     accum_out=accum_out),
                    deps=deps, reads=reads, writes=writes, laggy=laggy)

    def mm(out, lhsT, rhs, start, stop, reads=(), writes=(), deps=()):
        return P.op("tensor", lambda t: t.matmul(out, lhsT, rhs, start=start, stop=stop),
                    deps=deps, reads=reads, writes=writes)

    def tr(out, in_, reads=(), writes=(), deps=()):
        return P.op("tensor", lambda t: t.transpose(out, in_, id_sb),
                    deps=deps, reads=reads, writes=writes)

    def cp(stream, out, in_, reads=(), writes=(), deps=()):
        if stream == "scalar":
            return P.op(stream, lambda s_: s_.copy(out, in_),
                        deps=deps, reads=reads, writes=writes)
        return P.op(stream, lambda v: v.tensor_copy(out, in_),
                    deps=deps, reads=reads, writes=writes)


    def tr8(srcs, src_bufs, dst_even, dst_odd, dstbuf, evac_stream="vector"):
        for k in range(8):
            tr(PTs[k % 2][:, (k // 2) * 128:(k // 2 + 1) * 128], srcs[k],
               reads=(src_bufs[k], id_b), writes=(PTbs[k % 2],))
        cp(evac_stream, dst_even, PT[:, 0:512].rearrange("p (n m) -> p n m", n=4),
           reads=(PTb,), writes=(dstbuf,))
        cp(evac_stream, dst_odd, PT2[:, 0:512].rearrange("p (n m) -> p n m", n=4),
           reads=(PT2b,), writes=(dstbuf,))

    def rcp(out, in_, reads=(), writes=(), deps=()):
        return P.op("vector", lambda v: v.reciprocal(out, in_),
                    deps=deps, reads=reads, writes=writes, laggy=True)

    def red_absmax(out, in_, reads=(), writes=(), deps=()):
        return P.op("vector", lambda v: v.tensor_reduce(out, in_, axis=AX.X, op=ALU.max,
                                                        apply_absolute_value=True),
                    deps=deps, reads=reads, writes=writes, laggy=True)

    dma("qsync", id_sb[:, 0:128], ident_d, writes=(id_b,))
    dma("qsync", dk_sb, dk_d, writes=(dk_b,))
    dma("qsync", md_sb[:, 0:128], md_d, writes=(md_b,))
    dma("qsync", rel32_sb, rel.rearrange("a b -> (a b)")[None, :].partition_broadcast(128),
        writes=(rel32_b,))
    ts(rel32_sb, rel32_sb, SQD * CERF, None, ALU.mult, reads=(rel32_b,), writes=(rel32_b,))
    ts(bfar_sb, rel32_sb[:, 31:32], -MU * CERF, None, ALU.add, reads=(rel32_b,), writes=(bfar_b,))
    dma("qsync", tiny_sb[:, 0:1], clipx[None, :].partition_broadcast(128), writes=(tiny_b,))
    dma("qsync", tiny_sb[:, 1:2], clipd[None, :].partition_broadcast(128),
        reads=(tiny_b,), writes=(tiny_b,))
    act(sig_sb, tiny_sb[:, 0:2], AF.Sigmoid, reads=(tiny_b,), writes=(sig_b,))
    P.op("vector", lambda v: v.memset(muc_sb, -MU * CERF), writes=(muc_b,))
    dma("qsync", gb_sb[:, 0:1], gmm[0].unsqueeze(1), writes=(gb_b,))
    dma("qsync", gb_sb[:, 1:2], bet[0].unsqueeze(1), reads=(gb_b,), writes=(gb_b,))
    dma("qsync", gb_sb[:, 2:3], gmm[1].unsqueeze(1), reads=(gb_b,), writes=(gb_b,))
    dma("qsync", gb_sb[:, 3:4], bet[1].unsqueeze(1), reads=(gb_b,), writes=(gb_b,))
    dma("qsync", gb_sb[:, 4:5], bqk.unsqueeze(1), reads=(gb_b,), writes=(gb_b,))
    ts(gb_sb[:, 0:2], gb_sb[:, 0:2], 1.0 / T, None, ALU.mult, reads=(gb_b,), writes=(gb_b,))

    ctx = {}

    def ln_stats(x_tile, xb, mv, mvb, col):
        dmp, _n = ctx["st6"]
        P.op("vector", lambda v: v.tensor_scalar(
            dmp[0], x_tile, 1.0 / 1024.0, None, ALU.mult, ALU.add,
            accum_out=mv[:, col:col + 1]),
            reads=(xb,), writes=(ctx["dmpb0"], ctx["mv_m"]), laggy=True)
        P.op("scalar", lambda s: s.activation(
            dmp[1], x_tile, AF.Square, accum_out=mv[:, NT + col:NT + col + 1]),
            reads=(xb,), writes=(ctx["dmpb1"], ctx["mv_s"]), laggy=True)

    def ln_rstd(mv, mvb, rstd, rstdb, c0, n):
        w, wb = ctx["rw"]
        vpe = w[:, 0:n]; r0 = w[:, n:2 * n]; t1 = w[:, 2 * n:3 * n]
        varsl = mv[:, NT + c0:NT + c0 + n]
        meansl = mv[:, c0:c0 + n]
        ts(vpe, varsl, 1.0 / 1024.0, EPS, ALU.mult, ALU.add,
           reads=(ctx["mv_s"],), writes=(wb,))
        tt(t1, meansl, meansl, ALU.mult, reads=(ctx["mv_m"],), writes=(wb,))
        tt(vpe, vpe, t1, ALU.subtract, reads=(wb,), writes=(wb,))
        rcp(r0, vpe, reads=(wb,), writes=(wb,))
        P.op("scalar", lambda s: s.activation(r0, r0, AF.Sqrt), reads=(wb,), writes=(wb,))
        tt(t1, r0, r0, ALU.mult, reads=(wb,), writes=(wb,))
        tt(t1, t1, vpe, ALU.mult, reads=(wb,), writes=(wb,))
        ts(t1, t1, -0.5, 1.5, ALU.mult, ALU.add, reads=(wb,), writes=(wb,))
        tt(rstd[:, c0:c0 + n], r0, t1, ALU.mult, reads=(wb,), writes=(rstdb,))

    def phase_A():
        A.push()
        wq_sb = []; wk_sb = []; wv_sb = []; wo_sb = []
        wbufs = {}
        for nm, src, lst in (("wq", wq, wq_sb), ("wk", wk, wk_sb), ("wv", wv, wv_sb), ("wo", wo, wo_sb)):
            for dc in range(8):
                ap, b = A.bf16(1024, f"{nm}{dc}")
                lst.append(ap)
                wbufs[(nm, dc)] = b
                dma("qgp", ap, src[dc * 128:(dc + 1) * 128, :], writes=(b,))
        dt_sb = []; dq_sb = []
        dtb = Buf("dt"); dqb = Buf("dq")
        for h in range(NH):
            ap, _b = A.f32(128, f"dt{h}")
            dt_sb.append(ap)
            dma("qsync", ap, dt_d[h], reads=(dtb,), writes=(dtb,))
            ap2, _b2 = A.bf16(128, f"dq{h}")
            dq_sb.append(ap2)
            dma("qsync", ap2, dq_d[h], reads=(dqb,), writes=(dqb,))
        S_f = []; S_b = []; Sbf = []; Sbfb = []
        for h in range(NH):
            ap, b = A.f32(128, f"Sf{h}"); S_f.append(ap); S_b.append(b)
            ap2, b2 = A.bf16(128, f"Sb{h}"); Sbf.append(ap2); Sbfb.append(b2)
            P.op("vector", lambda v, ap=ap: v.memset(ap, 0.0), writes=(b,))
            P.op("vector", lambda v, ap=ap2: v.memset(ap, 0.0), writes=(b2,))
        dmp, _d = A.f32(2048, "lndump")
        ctx["st6"] = ((dmp[:, 0:1024], dmp[:, 1024:2048]), None)
        ctx["dmpb0"] = Buf("lndump0"); ctx["dmpb1"] = Buf("lndump1")
        ctx["mv_m"] = Buf("mv_mean"); ctx["mv_s"] = Buf("mv_ssq")
        ctx["rw"] = (RW, Buf("rw"))
        mv, mvb = MV, Buf("mv")
        rstd, rstdb = RSTD, Buf("rstd")
        x_sb = []; x_b = []
        for r in range(2):
            ap, b = A.f32(4096, f"xg{r}")
            x_sb.append(ap); x_b.append(b)
        xln, xlnb = A.bf16(4096, "xln")
        xlnT, xlnTb = A.bf16(4096, "xlnT")
        qT = []; qTb = []; kT = []; kTb = []
        for h in range(NH):
            ap, b = A.bf16(512, f"qT{h}"); qT.append(ap); qTb.append(b)
            ap, b = A.bf16(512, f"kT{h}"); kT.append(ap); kTb.append(b)
        ktok, ktokb = A.bf16(4096, "ktok")
        v_sb = []; v_b = []
        for t_ in range(4):
            ap, b = A.bf16(1024, f"v{t_}"); v_sb.append(ap); v_b.append(b)
        oT = []; oTb = []
        for ec in range(8):
            ap, b = A.bf16(512, f"oT{ec}"); oT.append(ap); oTb.append(b)
        qp_sb = []; qp_b = []
        at_sb = []; at_b = []
        for r in range(2):
            ap, b = A.bf16(128, f"qp{r}"); qp_sb.append(ap); qp_b.append(b)
            ap, b = A.bf16(128, f"at{r}"); at_sb.append(ap); at_b.append(b)

        for g in range(NG):
            xg, xgb = x_sb[g % 2], x_b[g % 2]
            dma("qsync", xg.rearrange("p (n m) -> p n m", n=4),
                x0[g * G:(g + 1) * G, :].rearrange("(n p) m -> p n m", p=128),
                writes=(xgb,))
            if PARTS < 2:
                dma("qsync", (out_d if stop_after == 1 else xA)[g * G:(g + 1) * G, :]
                    .rearrange("(n p) m -> p n m", p=128),
                    xg.rearrange("p (n m) -> p n m", n=4), reads=(xgb,))
                continue
            for t_ in range(4):
                ln_stats(xg[:, t_ * 1024:(t_ + 1) * 1024], xgb, mv, mvb, g * 4 + t_)
            ln_rstd(mv, mvb, rstd, rstdb, g * 4, 4)
            for t_ in range(4):
                c = g * 4 + t_
                if PARTS == 2:
                    ts(xg[:, t_ * 1024:(t_ + 1) * 1024], xg[:, t_ * 1024:(t_ + 1) * 1024],
                       mv[:, c:c + 1], rstd[:, c:c + 1], ALU.subtract, ALU.mult,
                       reads=(xgb,), preads=(ctx['mv_m'], rstdb), writes=(xgb,))
                elif PARTS == 21:
                    ts(xg[:, t_ * 1024:(t_ + 1) * 1024], xg[:, t_ * 1024:(t_ + 1) * 1024],
                       mv[:, c:c + 1], 1.0, ALU.subtract, ALU.mult,
                       reads=(xgb,), preads=(ctx['mv_m'], rstdb), writes=(xgb,))
                elif PARTS == 22:
                    ts(xg[:, t_ * 1024:(t_ + 1) * 1024], xg[:, t_ * 1024:(t_ + 1) * 1024],
                       0.0, rstd[:, c:c + 1], ALU.subtract, ALU.mult,
                       reads=(xgb,), preads=(ctx['mv_m'], rstdb), writes=(xgb,))
                elif PARTS == 23:
                    ts(xg[:, t_ * 1024:(t_ + 1) * 1024], xg[:, t_ * 1024:(t_ + 1) * 1024],
                       0.0, mv[:, NT + c:NT + c + 1], ALU.subtract, ALU.mult,
                       reads=(xgb,), preads=(ctx['mv_s'], rstdb), writes=(xgb,))
                elif PARTS in (24, 25, 26):
                    w_p, wb_p = ctx["rw"]
                    col = {24: t_, 25: 4 + t_, 26: 8 + t_}[PARTS]
                    ts(xg[:, t_ * 1024:(t_ + 1) * 1024], xg[:, t_ * 1024:(t_ + 1) * 1024],
                       0.0, w_p[:, col:col + 1], ALU.subtract, ALU.mult,
                       reads=(xgb,), preads=(wb_p, rstdb), writes=(xgb,))
                else:
                    ts(xln[:, t_ * 1024:(t_ + 1) * 1024], xg[:, t_ * 1024:(t_ + 1) * 1024],
                       mv[:, c:c + 1], rstd[:, c:c + 1], ALU.subtract, ALU.mult,
                       reads=(xgb,), preads=(ctx['mv_m'], rstdb), writes=(xlnb,))
            if PARTS < 3 or PARTS > 20:
                dma("qsync", (out_d if stop_after == 1 else xA)[g * G:(g + 1) * G, :]
                    .rearrange("(n p) m -> p n m", p=128),
                    xg.rearrange("p (n m) -> p n m", n=4), reads=(xgb,))
                continue
            for t_ in range(4):
                for dc in range(8):
                    tr(PT[:, dc * 128:(dc + 1) * 128],
                       xln[:, t_ * 1024 + dc * 128: t_ * 1024 + (dc + 1) * 128],
                       reads=(xlnb, id_b), writes=(PTb,))
                dstv = xlnT.rearrange("p (n m) -> p n m", n=8)[:, :, t_ * 128:(t_ + 1) * 128]
                cp("vector", dstv, PT.rearrange("p (n m) -> p n m", n=8),
                   reads=(PTb,), writes=(xlnTb,))
            if PARTS < 4:
                dma("qsync", (out_d if stop_after == 1 else xA)[g * G:(g + 1) * G, :]
                    .rearrange("(n p) m -> p n m", p=128),
                    xg.rearrange("p (n m) -> p n m", n=4), reads=(xgb,))
                continue
            for nm, wsb, dstl, dstbl, scale in (("q", wq_sb, qT, qTb, 128.0 ** -0.5),
                                                ("k", wk_sb, kT, kTb, 1.0)):
                for h in range(NH):
                    pb = PB[h % 2]; pbb = PBb[h % 2]
                    for dc in range(8):
                        mm(pb[:, 0:512], wsb[dc][:, h * 128:(h + 1) * 128],
                           xlnT[:, dc * 512:(dc + 1) * 512], dc == 0, dc == 7,
                           reads=(wbufs[("w" + nm, dc)], xlnTb), writes=(pbb,))
                    act(dstl[h], pb[:, 0:512], AF.Identity, scale=scale,
                        reads=(pbb,), writes=(dstbl[h],))
            if g == 0 and DBG:
                dump(xlnT, xlnTb, 4096)
                for h in range(NH):
                    dump(qT[h], qTb[h], 512)
                    dump(kT[h], kTb[h], 512)
                dump(dq_sb[0], dqb, 128)
                dump(dt_sb[0].bitcast(BF16), dtb, 256)
                dump(Sbf[0], Sbfb[0], 128)
            if PARTS < 5:
                dma("qsync", (out_d if stop_after == 1 else xA)[g * G:(g + 1) * G, :]
                    .rearrange("(n p) m -> p n m", p=128),
                    xg.rearrange("p (n m) -> p n m", n=4), reads=(xgb,))
                continue
            for t_ in range(4):
                for h in range(NH):
                    tr(PTs[h % 2][:, (h // 2) * 128:(h // 2 + 1) * 128],
                       kT[h][:, t_ * 128:(t_ + 1) * 128],
                       reads=(kTb[h], id_b), writes=(PTbs[h % 2],))
                for h in range(NH):
                    act(ktok[:, (t_ * 8 + h) * 128:(t_ * 8 + h + 1) * 128],
                        PTs[h % 2][:, (h // 2) * 128:(h // 2 + 1) * 128],
                        AF.Identity, scale=dk_sb[:, h:h + 1],
                        reads=(PTbs[h % 2], dk_b), writes=(ktokb,))
                for eh in range(2):
                    pb = PB[2 + eh]; pbb = PBb[2 + eh]
                    for dc in range(8):
                        mm(pb[:, 0:512], xlnT[:, dc * 512 + t_ * 128: dc * 512 + (t_ + 1) * 128],
                           wv_sb[dc][:, eh * 512:(eh + 1) * 512], dc == 0, dc == 7,
                           reads=(xlnTb, wbufs[("wv", dc)]), writes=(pbb,))
                    cp("vector", v_sb[t_][:, eh * 512:(eh + 1) * 512], pb[:, 0:512],
                       reads=(pbb,), writes=(v_b[t_],))
            if g == 0 and DBG:
                dump(ktok, ktokb, 4096)
                for t_ in range(4):
                    dump(v_sb[t_], v_b[t_], 1024)
            if PARTS < 6:
                dma("qsync", (out_d if stop_after == 1 else xA)[g * G:(g + 1) * G, :]
                    .rearrange("(n p) m -> p n m", p=128),
                    xg.rearrange("p (n m) -> p n m", n=4), reads=(xgb,))
                continue
            for t_ in range(4):
                for h in range(NH):
                    it = t_ * NH + h
                    pa = PB[4 + it % 2]; pab = PBb[4 + it % 2]
                    po = PB[2] if it % 2 == 0 else PB[3]
                    pob = PBb[2] if it % 2 == 0 else PBb[3]
                    mm(pa[:, 0:128], kT[h][:, t_ * 128:(t_ + 1) * 128],
                       qT[h][:, t_ * 128:(t_ + 1) * 128],
                       True, True, reads=(kTb[h], qTb[h]), writes=(pab,))
                    r = it % 2
                    tt(at_sb[r], pa[:, 0:128], dt_sb[h], ALU.mult,
                       reads=(pab, dtb), writes=(at_b[r],))
                    tt(qp_sb[r], qT[h][:, t_ * 128:(t_ + 1) * 128], dq_sb[h], ALU.mult,
                       reads=(qTb[h], dqb), writes=(qp_b[r],))
                    mm(po[:, 0:128], Sbf[h], qp_sb[r], True, False,
                       reads=(Sbfb[h], qp_b[r]), writes=(pob,))
                    mm(po[:, 0:128], v_sb[t_][:, h * 128:(h + 1) * 128], at_sb[r], False, True,
                       reads=(v_b[t_], at_b[r]), writes=(pob,))
                    act(oT[h][:, t_ * 128:(t_ + 1) * 128], po[:, 0:128], AF.Identity,
                        reads=(pob,), writes=(oTb[h],))
                    pk = PB[4 + (it + 1) % 2]; pkb = PBb[4 + (it + 1) % 2]
                    mm(pk[:, 0:128], ktok[:, (t_ * 8 + h) * 128:(t_ * 8 + h + 1) * 128],
                       v_sb[t_][:, h * 128:(h + 1) * 128], True, True,
                       reads=(ktokb, v_b[t_]), writes=(pkb,))
                    stt(S_f[h], S_f[h], GAMMAS[h] ** CH, pk[:, 0:128], ALU.mult, ALU.add,
                        reads=(pkb, S_b[h]), writes=(S_b[h],))
                    cp("vector", Sbf[h], S_f[h], reads=(S_b[h],), writes=(Sbfb[h],))
            if g == 0 and DBG:
                for ec in range(8):
                    dump(oT[ec], oTb[ec], 512)
            if PARTS < 7:
                dma("qsync", (out_d if stop_after == 1 else xA)[g * G:(g + 1) * G, :]
                    .rearrange("(n p) m -> p n m", p=128),
                    xg.rearrange("p (n m) -> p n m", n=4), reads=(xgb,))
                continue
            for t_ in range(4):
                for oh in range(2):
                    pw = PB[(t_ * 2 + oh) % 2]; pwb = PBb[(t_ * 2 + oh) % 2]
                    for ec in range(8):
                        mm(pw[:, 0:512], oT[ec][:, t_ * 128:(t_ + 1) * 128],
                           wo_sb[ec][:, oh * 512:(oh + 1) * 512], ec == 0, ec == 7,
                           reads=(oTb[ec], wbufs[("wo", ec)]), writes=(pwb,))
                    sl = xg[:, t_ * 1024 + oh * 512: t_ * 1024 + (oh + 1) * 512]
                    tt(sl, pw[:, 0:512], sl, ALU.add, reads=(pwb, xgb), writes=(xgb,))
            dst = (out_d if stop_after == 1 else xA)
            dma("qsync", dst[g * G:(g + 1) * G, :].rearrange("(n p) m -> p n m", p=128),
                xg.rearrange("p (n m) -> p n m", n=4), reads=(xgb,))
            if GROUP_BARRIER:
                P.barrier()
        P.barrier()
        A.pop()

    def gmax_collective(run, runb, dst, dstb, sigcol):
        w, wb = tiny_sb, tiny_b
        P.op("gpsimd", lambda g: g.tensor_reduce(w[0:1, 2:3], run, axis=AX.C, op=ALU.max),
             reads=(runb,), writes=(wb,))
        dma("qgp", cc_in[:], w[0:1, 2:3], reads=(wb,))
        cc_ev = P.op("gpsimd", lambda g: g.collective_compute(
            "AllReduce", ALU.max, replica_groups=[list(range(NCORES))],
            ins=[cc_in[:]], outs=[cc_out[:]]),
            deps=[P.last("qgp")])
        dma("qgp", w[0:1, 3:4], cc_out[:], reads=(wb,), writes=(wb,), deps=[cc_ev])
        tt(w[0:1, 4:5], w[0:1, 3:4], sig_sb[0:1, sigcol:sigcol + 1], ALU.mult,
           reads=(wb, sig_b), writes=(wb,))
        ts(w[0:1, 5:6], w[0:1, 4:5], 1.0 / QMAX, None, ALU.mult, reads=(wb,), writes=(wb,))
        rcp(w[0:1, 6:7], w[0:1, 5:6], reads=(wb,), writes=(wb,))
        dma("qgp", bc_scr[:], w[0:1, 5:7], reads=(wb,))
        dma("qgp", dst, bc_scr[None, :].partition_broadcast(128), writes=(dstb,),
            deps=[P.last("qgp")])

    def phase_FFN(src, dst, first):
        A.push()
        dmp, _d = A.f32(2048, "lndump")
        ctx["st6"] = ((dmp[:, 0:1024], dmp[:, 1024:2048]), None)
        ctx["dmpb0"] = Buf("lndump0"); ctx["dmpb1"] = Buf("lndump1")
        ctx["mv_m"] = Buf("mv_mean"); ctx["mv_s"] = Buf("mv_ssq")
        ctx["rw"] = (RW, Buf("rw"))
        mv, mvb = MV, Buf("mv")
        rstd, rstdb = RSTD, Buf("rstd")
        rs2, rs2b = RS2, Buf("rs2")
        sxv, sxvb = SXV, Buf("sxv")
        shv, shvb = SHV, Buf("shv")
        gsc, gscb = GSC, Buf("gsc")
        usc, uscb = USC, Buf("usc")
        xmax, xmaxb = XMAX, Buf("xmaxrun")
        hmax, hmaxb = HMAX, Buf("hmaxrun")
        x_sb = []; x_b = []
        for r in range(2):
            ap, b = A.f32(4096, f"fx{r}")
            x_sb.append(ap); x_b.append(b)
        xlt, xltb = A.f32(1024, "xln_tmp")
        w_, wb_ = ctx["rw"]

        A.push()
        wgT, wgTb = A.bf16(8 * II, "wgT")
        wuT, wuTb = A.bf16(8 * II, "wuT")
        A.push()
        if first:
            wtmp = []; wtb = []
            for r in range(2):
                ap, b = A.f32(1024, f"wtmp{r}"); wtmp.append(ap); wtb.append(b)
            wi, wib = A.bf16(1024, "wint")
            for nm, src_w, dstT, dstTb, ssb, ssbuf, cache in (
                    ("g", wg, wgT, wgTb, swg_sb, swg_b, wgt_c),
                    ("u", wu, wuT, wuTb, swu_sb, swu_b, wut_c)):
                for ic in range(NI):
                    r = ic % 2
                    dma("qgp", wtmp[r], src_w[ic * 128:(ic + 1) * 128, :], writes=(wtb[r],))
                    red_absmax(w_[:, 16:17], wtmp[r], reads=(wtb[r],), writes=(wb_,))
                    ts(ssb[:, ic:ic + 1], w_[:, 16:17], 1.0 / QMAX, None, ALU.mult,
                       reads=(wb_,), writes=(ssbuf,))
                    rcp(w_[:, 17:18], w_[:, 16:17], reads=(wb_,), writes=(wb_,))
                    ts(w_[:, 17:18], w_[:, 17:18], QMAX, None, ALU.mult, reads=(wb_,), writes=(wb_,))
                    ts(wtmp[r], wtmp[r], w_[:, 17:18], MAGIC, ALU.mult, ALU.add,
                       reads=(wtb[r],), preads=(wb_,), writes=(wtb[r],))
                    ts(wi, wtmp[r], -MAGIC, None, ALU.add, reads=(wtb[r],), writes=(wib,))
                    srcs = [wi[:, dc * 128:(dc + 1) * 128] for dc in range(8)]
                    v8 = dstT.rearrange("p (n m) -> p n m", n=8)
                    tr8(srcs, [wib] * 8,
                        v8[:, 0::2, ic * 128:(ic + 1) * 128],
                        v8[:, 1::2, ic * 128:(ic + 1) * 128], dstTb, evac_stream="scalar")
                for dc in range(8):
                    dma("qscalar", cache[dc], dstT[:, dc * II:(dc + 1) * II], reads=(dstTb,))
        else:
            for dc in range(8):
                dma("qgp", wgT[:, dc * II:(dc + 1) * II], wgt_c[dc], reads=(wgTb,), writes=(wgTb,))
                dma("qgp", wuT[:, dc * II:(dc + 1) * II], wut_c[dc], reads=(wuTb,), writes=(wuTb,))

        P.op("vector", lambda v: v.memset(xmax[:, 0:1], 0.0), writes=(xmaxb,))
        for g in range(NG):
            xg, xgb = x_sb[g % 2], x_b[g % 2]
            dma("qsync", xg.rearrange("p (n m) -> p n m", n=4),
                src[g * G:(g + 1) * G, :].rearrange("(n p) m -> p n m", p=128), writes=(xgb,))
            for t_ in range(4):
                ln_stats(xg[:, t_ * 1024:(t_ + 1) * 1024], xgb, mv, mvb, g * 4 + t_)
            ln_rstd(mv, mvb, rstd, rstdb, g * 4, 4)
            for t_ in range(4):
                c = g * 4 + t_
                ts(xlt, xg[:, t_ * 1024:(t_ + 1) * 1024],
                   mv[:, c:c + 1], rstd[:, c:c + 1], ALU.subtract, ALU.mult,
                   reads=(xgb,), preads=(ctx['mv_m'], rstdb), writes=(xltb,))
                red_absmax(w_[:, 18:19], xlt, reads=(xltb,), writes=(wb_,))
                tt(xmax[:, 0:1], xmax[:, 0:1], w_[:, 18:19], ALU.max,
                   reads=(wb_, xmaxb), writes=(xmaxb,))
        gmax_collective(xmax[:, 0:1], xmaxb, sxv, sxvb, 0)
        ts(rs2, rstd, sxv[:, 1:2], None, ALU.mult, reads=(rstdb, sxvb), writes=(rs2b,))
        ts(gsc, swg_sb, sxv[:, 0:1], None, ALU.mult, reads=(swg_b, sxvb), writes=(gscb,))
        ts(usc, swu_sb, sxv[:, 0:1], None, ALU.mult, reads=(swu_b, sxvb), writes=(uscb,))
        P.barrier()
        A.pop()

        A.push()
        xi, xib = A.bf16(4096, "xint")
        xiT, xiTb = A.bf16(4096, "xintT")
        gu = []; gub = []
        for r in range(2):
            ap, b = A.f32(512, f"gu{r}"); gu.append(ap); gub.append(b)
        hg = []; hgb = []
        for r in range(2):
            ap, b = A.f32(11 * 512, f"hgrp{r}"); hg.append(ap); hgb.append(b)
        P.op("vector", lambda v: v.memset(hmax[:, 0:1], 0.0), writes=(hmaxb,))
        for g in range(NG):
            xg, xgb = x_sb[g % 2], x_b[g % 2]
            dma("qsync", xg.rearrange("p (n m) -> p n m", n=4),
                src[g * G:(g + 1) * G, :].rearrange("(n p) m -> p n m", p=128), writes=(xgb,))
            for t_ in range(4):
                c = g * 4 + t_
                sl = xi[:, t_ * 1024:(t_ + 1) * 1024]
                ts(xlt, xg[:, t_ * 1024:(t_ + 1) * 1024],
                   mv[:, c:c + 1], rs2[:, c:c + 1], ALU.subtract, ALU.mult,
                   reads=(xgb,), preads=(ctx['mv_m'], rs2b), writes=(xltb,))
                ts(xlt, xlt, MAGIC, MAGIC + QMAX, ALU.add, ALU.min,
                   reads=(xltb,), writes=(xltb,))
                ts(sl, xlt, MAGIC - QMAX - 1.0, -MAGIC, ALU.max, ALU.add,
                   reads=(xltb,), writes=(xib,))
            for t_ in range(4):
                srcs = [xi[:, t_ * 1024 + dc * 128: t_ * 1024 + (dc + 1) * 128] for dc in range(8)]
                v8 = xiT.rearrange("p (n m) -> p n m", n=8)
                tr8(srcs, [xib] * 8,
                    v8[:, 0::2, t_ * 128:(t_ + 1) * 128],
                    v8[:, 1::2, t_ * 128:(t_ + 1) * 128], xiTb)
            for ic in range(NI):
                pg = PB[ic % 2]; pgb = PBb[ic % 2]
                pu = PB[2 + ic % 2]; pub = PBb[2 + ic % 2]
                for dc in range(8):
                    mm(pg[:, 0:512], wgT[:, dc * II + ic * 128: dc * II + (ic + 1) * 128],
                       xiT[:, dc * 512:(dc + 1) * 512], dc == 0, dc == 7,
                       reads=(wgTb, xiTb), writes=(pgb,))
                for dc in range(8):
                    mm(pu[:, 0:512], wuT[:, dc * II + ic * 128: dc * II + (ic + 1) * 128],
                       xiT[:, dc * 512:(dc + 1) * 512], dc == 0, dc == 7,
                       reads=(wuTb, xiTb), writes=(pub,))
                gr = ic % 2
                act(gu[gr], pg[:, 0:512], AF.Silu, scale=gsc[:, ic:ic + 1],
                    reads=(pgb, gscb), writes=(gub[gr],))
                half = ic // 11
                hsl = hg[half][:, (ic % 11) * 512:(ic % 11 + 1) * 512]
                stt(hsl, pu[:, 0:512], usc[:, ic:ic + 1], gu[gr], ALU.mult, ALU.mult,
                    reads=(pub, gub[gr]), preads=(uscb,), writes=(hgb[half],))
                red_absmax(w_[:, 19:20], hsl, reads=(hgb[half],), writes=(wb_,))
                tt(hmax[:, 0:1], hmax[:, 0:1], w_[:, 19:20], ALU.max,
                   reads=(wb_, hmaxb), writes=(hmaxb,))
                if ic % 11 == 10:
                    dma("qscalar",
                        h_scr[half * 11 * 128:(half + 1) * 11 * 128, g * G:(g + 1) * G]
                        .rearrange("(n p) m -> p n m", p=128),
                        hg[half].rearrange("p (n m) -> p n m", n=11), reads=(hgb[half],))
            if GROUP_BARRIER:
                P.barrier()
        gmax_collective(hmax[:, 0:1], hmaxb, shv, shvb, 1)

        P.barrier()
        A.pop()
        A.pop()
        wdT, wdTb = A.bf16(NI * 1024, "wdT")
        hs0, hs0b = A.f32(NI * 512, "hs")
        hs = [hs0, hs0]; hsb = [hs0b, hs0b]
        hq, hqb = A.bf16(NI * 512, "hq")
        dtmp, dtmpb = A.f32(512, "dtmp")
        if first:
            wtmp2 = []; wtb2 = []
            for r in range(2):
                ap, b = A.f32(II, f"wd{r}"); wtmp2.append(ap); wtb2.append(b)
            wi2, wi2b = A.bf16(II, "wdint")
            for oc in range(8):
                r = oc % 2
                dma("qgp", wtmp2[r], wd[oc * 128:(oc + 1) * 128, :], writes=(wtb2[r],))
                red_absmax(w_[:, 16:17], wtmp2[r], reads=(wtb2[r],), writes=(wb_,))
                ts(w_[:, 20:21], w_[:, 16:17], 1.0 / QMAX, None, ALU.mult,
                   reads=(wb_,), writes=(wb_,))
                rcp(w_[:, 17:18], w_[:, 16:17], reads=(wb_,), writes=(wb_,))
                ts(w_[:, 17:18], w_[:, 17:18], QMAX, None, ALU.mult, reads=(wb_,), writes=(wb_,))
                ts(wtmp2[r], wtmp2[r], w_[:, 17:18], MAGIC, ALU.mult, ALU.add,
                   reads=(wtb2[r],), preads=(wb_,), writes=(wtb2[r],))
                ts(wtmp2[r], wtmp2[r], -MAGIC, None, ALU.add, reads=(wtb2[r],), writes=(wtb2[r],))
                ts(wi2, wtmp2[r], w_[:, 20:21], None, ALU.mult,
                   reads=(wtb2[r],), preads=(wb_,), writes=(wi2b,))
                for ic in range(NI):
                    tr(PTs[ic % 2][:, (ic % 8) // 2 * 128:((ic % 8) // 2 + 1) * 128],
                       wi2[:, ic * 128:(ic + 1) * 128],
                       reads=(wi2b, id_b), writes=(PTbs[ic % 2],))
                    if ic % 8 == 7 or ic == NI - 1:
                        base = ic - ic % 8
                        nb = ic % 8 + 1
                        v8 = wdT.rearrange("p (n m) -> p n m", m=1024)
                        ne = (nb + 1) // 2
                        no = nb // 2
                        cp("scalar", v8[:, base:base + nb:2, oc * 128:(oc + 1) * 128],
                           PT[:, 0:ne * 128].rearrange("p (n m) -> p n m", n=ne),
                           reads=(PTb,), writes=(wdTb,))
                        cp("scalar", v8[:, base + 1:base + nb:2, oc * 128:(oc + 1) * 128],
                           PT2[:, 0:no * 128].rearrange("p (n m) -> p n m", n=no),
                           reads=(PT2b,), writes=(wdTb,))
            for ic in range(NI):
                dma("qscalar", wdt_c[ic], wdT[:, ic * 1024:(ic + 1) * 1024], reads=(wdTb,))
        else:
            for ic in range(NI):
                dma("qgp", wdT[:, ic * 1024:(ic + 1) * 1024], wdt_c[ic],
                    reads=(wdTb,), writes=(wdTb,))

        for g in range(NG):
            hr = g % 2
            xg, xgb = x_sb[g % 2], x_b[g % 2]
            dma("qgp", hs[hr].rearrange("p (n m) -> p n m", n=NI),
                h_scr[:, g * G:(g + 1) * G].rearrange("(n p) m -> p n m", p=128),
                writes=(hsb[hr],))
            dma("qsync", xg.rearrange("p (n m) -> p n m", n=4),
                src[g * G:(g + 1) * G, :].rearrange("(n p) m -> p n m", p=128), writes=(xgb,))
            for ic in range(NI):
                sl = hs[hr][:, ic * 512:(ic + 1) * 512]
                qsl = hq[:, ic * 512:(ic + 1) * 512]
                ts(sl, sl, shv[:, 1:2], MAGIC, ALU.mult, ALU.add,
                   reads=(hsb[hr], shvb), writes=(hsb[hr],))
                ts(sl, sl, MAGIC + QMAX, MAGIC - QMAX - 1.0, ALU.min, ALU.max,
                   reads=(hsb[hr],), writes=(hsb[hr],))
                ts(qsl, sl, -MAGIC, None, ALU.add, reads=(hsb[hr],), writes=(hqb,))
            for t_ in range(4):
                for oh in range(2):
                    pd = PB[(t_ * 2 + oh) % 4]; pdb = PBb[(t_ * 2 + oh) % 4]
                    for ic in range(NI):
                        mm(pd[:, 0:512], hq[:, ic * 512 + t_ * 128: ic * 512 + (t_ + 1) * 128],
                           wdT[:, ic * 1024 + oh * 512: ic * 1024 + (oh + 1) * 512],
                           ic == 0, ic == NI - 1,
                           reads=(hqb, wdTb), writes=(pdb,))
                    sl = xg[:, t_ * 1024 + oh * 512: t_ * 1024 + (oh + 1) * 512]
                    stt(sl, pd[:, 0:512], shv[:, 0:1], sl, ALU.mult, ALU.add,
                        reads=(pdb, shvb, xgb), writes=(xgb,))
            dma("qsync", dst[g * G:(g + 1) * G, :].rearrange("(n p) m -> p n m", p=128),
                xg.rearrange("p (n m) -> p n m", n=4), reads=(xgb,))
            if GROUP_BARRIER:
                P.barrier()
        P.barrier()
        A.pop()

    def phase_C():
        A.push()
        dmp, _d = A.f32(2048, "lndump")
        ctx["st6"] = ((dmp[:, 0:1024], dmp[:, 1024:2048]), None)
        ctx["dmpb0"] = Buf("lndump0"); ctx["dmpb1"] = Buf("lndump1")
        ctx["mv_m"] = Buf("mv_mean"); ctx["mv_s"] = Buf("mv_ssq")
        ctx["rw"] = (RW, Buf("rw"))
        mv, mvb = MV, Buf("mv")
        rstd, rstdb = RSTD, Buf("rstd")
        qTa, qTab = A.bf16(T, "qT_all")
        kTa, kTab = A.bf16(T, "kT_all")
        tb_, tbb = A.f32(256, "toep")
        bvb_sb, bvb_b = A.f32(1024, "bvbcast")
        x_sb = []; x_b = []
        for r in range(2):
            ap, b = A.f32(4096, f"cx{r}")
            x_sb.append(ap); x_b.append(b)

        bmk, bmkb = A.bf16(256, "bmask")
        P.op("vector", lambda v: v.memset(tb_, 0.0), writes=(tbb,))
        for b in range(NBUCK):
            dma("qsync", bmk, bm_d[b], reads=(bmkb,), writes=(bmkb,))
            stt(tb_, bmk, rel32_sb[:, b:b + 1], tb_, ALU.mult, ALU.add,
                reads=(bmkb, tbb), preads=(rel32_b,), writes=(tbb,))
        dma("qsync", bvb_sb, bvb[None, :].partition_broadcast(128), writes=(bvb_b,))

        for g in range(NG):
            xg, xgb = x_sb[g % 2], x_b[g % 2]
            dma("qsync", xg.rearrange("p (n m) -> p n m", n=4),
                xB[g * G:(g + 1) * G, :].rearrange("(n p) m -> p n m", p=128), writes=(xgb,))
            for t_ in range(4):
                ln_stats(xg[:, t_ * 1024:(t_ + 1) * 1024], xgb, mv, mvb, g * 4 + t_)
            ln_rstd(mv, mvb, rstd, rstdb, g * 4, 4)

        A.push()
        xln, xlnb = A.bf16(4096, "xln")
        xlnT, xlnTb = A.bf16(4096, "xlnT")
        wqk_sb, wqk_b = A.bf16(8 * 128, "wqk")
        for dc in range(8):
            dma("qgp", wqk_sb[:, dc * 128:(dc + 1) * 128], wqk[dc * 128:(dc + 1) * 128, :],
                reads=(wqk_b,), writes=(wqk_b,))
        wvb_sb = []; wvb_b = []
        for r in range(2):
            ap, b = A.bf16(1024, f"wvb{r}"); wvb_sb.append(ap); wvb_b.append(b)
        stmp = []; stmpb = []
        for r in range(2):
            ap, b = A.f32(512, f"qks{r}"); stmp.append(ap); stmpb.append(b)
        vt1, vt1b = A.f32(512, "vtmp")
        vout = []; voutb = []
        for t_ in range(4):
            ap, b = A.bf16(1024, f"vout{t_}"); vout.append(ap); voutb.append(b)
        for g in range(NG):
            xg, xgb = x_sb[g % 2], x_b[g % 2]
            dma("qsync", xg.rearrange("p (n m) -> p n m", n=4),
                xB[g * G:(g + 1) * G, :].rearrange("(n p) m -> p n m", p=128), writes=(xgb,))
            for t_ in range(4):
                c = g * 4 + t_
                ts(xln[:, t_ * 1024:(t_ + 1) * 1024], xg[:, t_ * 1024:(t_ + 1) * 1024],
                   mv[:, c:c + 1], rstd[:, c:c + 1], ALU.subtract, ALU.mult,
                   reads=(xgb,), preads=(ctx['mv_m'], rstdb), writes=(xlnb,))
            for t_ in range(4):
                srcs = [xln[:, t_ * 1024 + dc * 128: t_ * 1024 + (dc + 1) * 128] for dc in range(8)]
                v8 = xlnT.rearrange("p (n m) -> p n m", n=8)
                tr8(srcs, [xlnb] * 8,
                    v8[:, 0::2, t_ * 128:(t_ + 1) * 128],
                    v8[:, 1::2, t_ * 128:(t_ + 1) * 128], xlnTb)
            pq = PB[4 + g % 2]; pqb = PBb[4 + g % 2]
            for dc in range(8):
                mm(pq[:, 0:512], wqk_sb[:, dc * 128:(dc + 1) * 128],
                   xlnT[:, dc * 512:(dc + 1) * 512],
                   dc == 0, dc == 7, reads=(wqk_b, xlnTb), writes=(pqb,))
            sr = g % 2
            act(stmp[sr], pq[:, 0:512], AF.Silu, bias=gb_sb[:, 4:5],
                reads=(pqb, gb_b), writes=(stmpb[sr],))
            ts(qTa[:, g * G:(g + 1) * G], stmp[sr], gb_sb[:, 0:1], gb_sb[:, 1:2],
               ALU.mult, ALU.add, reads=(stmpb[sr],), preads=(gb_b,), writes=(qTab,))
            ts(kTa[:, g * G:(g + 1) * G], stmp[sr], gb_sb[:, 2:3], gb_sb[:, 3:4],
               ALU.mult, ALU.add, reads=(stmpb[sr],), preads=(gb_b,), writes=(kTab,))
            for eh in range(2):
                for dc in range(8):
                    r = dc % 2
                    dma("qgp", wvb_sb[r], wvb[dc * 128:(dc + 1) * 128, :],
                        reads=(wvb_b[r],), writes=(wvb_b[r],))
                    for t_ in range(4):
                        mm(PB[t_][:, 0:512],
                           xlnT[:, dc * 512 + t_ * 128: dc * 512 + (t_ + 1) * 128],
                           wvb_sb[r][:, eh * 512:(eh + 1) * 512], dc == 0, dc == 7,
                           reads=(xlnTb, wvb_b[r]), writes=(PBb[t_],))
                for t_ in range(4):
                    tt(vt1, PB[t_][:, 0:512], bvb_sb[:, eh * 512:(eh + 1) * 512], ALU.add,
                       reads=(PBb[t_], bvb_b), writes=(vt1b,))
                    act(vout[t_][:, eh * 512:(eh + 1) * 512], vt1, AF.Silu,
                        reads=(vt1b,), writes=(voutb[t_],))
            for t_ in range(4):
                dma("qscalar", v_scr[g * 4 + t_], vout[t_], reads=(voutb[t_],))
            if GROUP_BARRIER:
                P.barrier()
        P.barrier()
        A.pop()

        NVR = 6
        vr = []; vrb = []
        for r in range(NVR):
            ap, b = A.bf16(1024, f"vr{r}"); vr.append(ap); vrb.append(b)
        at2 = []; at2b = []
        et = []; etb = []
        for r in range(2):
            ap, b = A.bf16(256, f"attnT{r}"); at2.append(ap); at2b.append(b)
            ap, b = A.f32(256, f"erf{r}"); et.append(ap); etb.append(b)
        x2, x2b = A.f32(2048, "x2slab")
        NSL = T // 256
        for si in range(NSL):
            pos = [[PB[2], PB[3]], [PB[4], PB[5]]]
            posb = [[PBb[2], PBb[3]], [PBb[4], PBb[5]]]
            dma("qsync", x2.rearrange("p (n m) -> p n m", n=2),
                xB[si * 256:(si + 1) * 256, :].rearrange("(n p) m -> p n m", p=128),
                writes=(x2b,))
            started = [[False, False], [False, False]]
            njt = 2 * si + 2
            for jt in range(min(njt, NVR)):
                vri = jt % NVR
                dma("qgp", vr[vri], v_scr[jt], reads=(vrb[vri],), writes=(vrb[vri],))
            for jt in range(njt):
                ps = PB[jt % 2]; psb = PBb[jt % 2]
                mm(ps[:, 0:256], kTa[:, jt * 128:(jt + 1) * 128],
                   qTa[:, si * 256:(si + 1) * 256],
                   True, True, reads=(kTab, qTab), writes=(psb,))
                ar = jt % 2
                er = jt % 2
                near = jt >= 2 * si - 1
                if not near:
                    act(et[er], ps[:, 0:256], AF.Erf, bias=bfar_sb[:, 0:1], scale=CERF,
                        reads=(psb, bfar_b), writes=(etb[er],))
                    ts(at2[ar], et[er], 0.5, 0.5, ALU.mult, ALU.add,
                       reads=(etb[er],), writes=(at2b[ar],))
                else:
                    for iq in range(2):
                        d0 = 256 * si + 128 * iq - 128 * jt
                        if d0 < 0:
                            continue
                        psl = ps[:, iq * 128:(iq + 1) * 128]
                        esl = et[er][:, iq * 128:(iq + 1) * 128]
                        asl = at2[ar][:, iq * 128:(iq + 1) * 128]
                        if d0 in (0, 128):
                            tsl = tb_[:, 0:128] if d0 == 0 else tb_[:, 128:256]
                            stt(esl, psl, CERF, tsl, ALU.mult, ALU.add,
                                reads=(psb, tbb), writes=(etb[er],))
                            act(esl, esl, AF.Erf, bias=muc_sb[:, 0:1],
                                reads=(etb[er], muc_b), writes=(etb[er],))
                            if d0 == 0:
                                ts(esl, esl, 0.5, 0.5, ALU.mult, ALU.add,
                                   reads=(etb[er],), writes=(etb[er],))
                                tt(asl, esl, md_sb[:, 0:128], ALU.mult,
                                   reads=(etb[er], md_b), writes=(at2b[ar],))
                            else:
                                ts(asl, esl, 0.5, 0.5, ALU.mult, ALU.add,
                                   reads=(etb[er],), writes=(at2b[ar],))
                        else:
                            act(esl, psl, AF.Erf, bias=bfar_sb[:, 0:1], scale=CERF,
                                reads=(psb, bfar_b), writes=(etb[er],))
                            ts(asl, esl, 0.5, 0.5, ALU.mult, ALU.add,
                               reads=(etb[er],), writes=(at2b[ar],))
                vri = jt % NVR
                for iq in range(2):
                    if 128 * jt > 256 * si + 128 * iq:
                        continue
                    last = jt == min(njt - 1, 2 * si + iq)
                    for eh in range(2):
                        mm(pos[iq][eh][:, 0:512], at2[ar][:, iq * 128:(iq + 1) * 128],
                           vr[vri][:, eh * 512:(eh + 1) * 512],
                           not started[iq][eh], last,
                           reads=(at2b[ar], vrb[vri]), writes=(posb[iq][eh],))
                        started[iq][eh] = True
                if jt + NVR < njt:
                    nx = jt + NVR
                    dma("qgp", vr[nx % NVR], v_scr[nx], reads=(vrb[nx % NVR],),
                        writes=(vrb[nx % NVR],))
                if SLAB_BARRIER:
                    P.barrier()
            for iq in range(2):
                for eh in range(2):
                    sl = x2[:, iq * 1024 + eh * 512: iq * 1024 + (eh + 1) * 512]
                    tt(sl, pos[iq][eh][:, 0:512], sl, ALU.add,
                       reads=(posb[iq][eh], x2b), writes=(x2b,))
            dst = (out_d if stop_after == 3 else xC)
            dma("qsync", dst[si * 256:(si + 1) * 256, :].rearrange("(n p) m -> p n m", p=128),
                x2.rearrange("p (n m) -> p n m", n=2), reads=(x2b,))
            if SLAB_BARRIER:
                P.barrier()
        P.barrier()
        A.pop()

    phase_A()
    if stop_after >= 2:
        phase_FFN(xA, out_d if stop_after == 2 else xB, first=True)
    if stop_after >= 3:
        phase_C()
    if stop_after >= 4:
        phase_FFN(xC, out_d, first=False)

    P.finalize(nc)
    with nc.Block() as block:
        for engname in ("sync", "scalar", "vector", "tensor", "gpsimd"):
            getattr(block, engname)(lambda eng, n=engname: P.emit_engine(eng, n))
    stack.close()
    return nc


def kernel(L0_attn_a_hidden_states, L0_attn_a_rel_pos, L0_attn_a_Wq, L0_attn_a_Wk,
           L0_attn_a_Wv, L0_attn_a_Wo, L0_attn_b_Wqk, L0_attn_b_bqk, L0_attn_b_gamma,
           L0_attn_b_beta, L0_attn_b_Wv, L0_attn_b_bv, L0_attn_b_rel_emb,
           L0_ffn_clip_x, L0_ffn_clip_down, L0_ffn_w_gate, L0_ffn_w_up, L0_ffn_w_down,
           stop_after=4):
    consts = host_constants()
    nc = build_program(stop_after=stop_after)
    shared = dict(
        wq=np.ascontiguousarray(L0_attn_a_Wq, np.float32),
        wk=np.ascontiguousarray(L0_attn_a_Wk, np.float32),
        wv=np.ascontiguousarray(L0_attn_a_Wv, np.float32),
        wo=np.ascontiguousarray(L0_attn_a_Wo, np.float32),
        wqk=np.ascontiguousarray(L0_attn_b_Wqk, np.float32),
        bqk=np.ascontiguousarray(L0_attn_b_bqk, np.float32),
        gmm=np.ascontiguousarray(L0_attn_b_gamma, np.float32),
        bet=np.ascontiguousarray(L0_attn_b_beta, np.float32),
        wvb=np.ascontiguousarray(L0_attn_b_Wv, np.float32),
        bvb=np.ascontiguousarray(L0_attn_b_bv, np.float32),
        rel=np.ascontiguousarray(L0_attn_b_rel_emb, np.float32),
        clipx=np.ascontiguousarray(L0_ffn_clip_x, np.float32),
        clipd=np.ascontiguousarray(L0_ffn_clip_down, np.float32),
        wg=np.ascontiguousarray(L0_ffn_w_gate, np.float32),
        wu=np.ascontiguousarray(L0_ffn_w_up, np.float32),
        wd=np.ascontiguousarray(L0_ffn_w_down, np.float32),
        ident=consts["ident"],
        dt_all=consts["dt_all"],
        dq_all=consts["dq_all"],
        dk_all=consts["dk_all"],
        bmasks=consts["bmasks"],
        mdiag=consts["mdiag"],
    )
    xs = np.ascontiguousarray(L0_attn_a_hidden_states, np.float32)
    in_maps = [dict(shared, x0=np.ascontiguousarray(xs[b])) for b in range(8)]
    res = run_bass_kernel_spmd(nc, in_maps, core_ids=list(range(8)))
    if DBG:
        np.save("/root/problem/scratch/dbg0.npy", np.asarray(res.results[0]["dbg"]))
    return np.stack([np.asarray(res.results[b]["out"]) for b in range(8)], 0)
